# version 04
# baseline (speedup 1.0000x reference)
import math
import contextlib
import numpy as np

import concourse.bass as bass
import concourse.mybir as mybir
from concourse.bass_utils import run_bass_kernel_spmd

F32 = mybir.dt.float32
BF16 = mybir.dt.bfloat16
I32 = mybir.dt.int32
AF = mybir.ActivationFunctionType
ALU = mybir.AluOpType
AXX = mybir.AxisListType.X

ENGS = ("t", "v", "s", "g", "y")


def _names_of(ap):
    try:
        return {ap.tensor.name}
    except Exception:
        return set()


class Chan:

    def __init__(self, env, name, step=1):
        self.env = env
        self.sem = env.es.enter_context(env.nc.semaphore(name))
        self.step = step
        self.count = 0

    def inc(self, inst, n=1):
        inst.then_inc(self.sem, self.step * n)
        self.count += self.step * n
        return self.count

    def mark(self):
        return self.count

    def wait(self, eng, upto=None):
        v = self.count if upto is None else upto
        if v > 0:
            eng.raw.wait_ge(self.sem, v)

    def inc_seq(self, eng, n=1):
        eng.drain()
        eng.raw.sem_inc(self.sem, self.step * n)
        self.count += self.step * n
        return self.count


class Eng:

    def __init__(self, env, key, raw):
        self.env = env
        self.key = key
        self.raw = raw
        self.written = set()
        self.read = set()
        self.n_ops = 0

    def _pre(self, reads, writes):
        r = set()
        for ap in reads:
            r |= _names_of(ap)
        w = set()
        for ap in writes:
            w |= _names_of(ap)
        if (r & self.written) or (w & self.written) or (w & self.read):
            self.drain()
        self.read |= r
        self.written |= w
        self.n_ops += 1

    def drain(self):
        if self.n_ops > 0:
            self.raw.drain()
        self.written.clear()
        self.read.clear()
        self.n_ops = 0

    def tt(self, out, a, b, op):
        self._pre([a, b], [out])
        return self.raw.tensor_tensor(out, a, b, op)

    def ts(self, out, a, s1, s2=None, op0=ALU.mult, op1=ALU.bypass, accum_out=None):
        rd = [a] + ([s1] if isinstance(s1, bass.AP) else []) + ([s2] if isinstance(s2, bass.AP) else [])
        wr = [out] + ([accum_out] if accum_out is not None else [])
        self._pre(rd, wr)
        if s2 is None and op1 is ALU.bypass and accum_out is None:
            return self.raw.tensor_single_scalar(out, a, s1, op0)
        return self.raw.tensor_scalar(out, a, s1, s2, op0, op1, accum_out=accum_out)

    def stt(self, out, in0, scalar, in1, op0, op1):
        rd = [in0, in1] + ([scalar] if isinstance(scalar, bass.AP) else [])
        self._pre(rd, [out])
        return self.raw.scalar_tensor_tensor(out, in0, scalar, in1, op0, op1)

    def copy(self, out, a):
        self._pre([a], [out])
        return self.raw.tensor_copy(out, a)

    def memset(self, out, v):
        self._pre([], [out])
        return self.raw.memset(out, v)

    def red(self, out, a, op, absolute=False, negate=None):
        self._pre([a], [out])
        return self.raw.tensor_reduce(out, a, AXX, op, apply_absolute_value=absolute, negate=negate)

    def scan(self, out, d0, d1, initial, op0, op1):
        rd = [d0, d1] + ([initial] if isinstance(initial, bass.AP) else [])
        self._pre(rd, [out])
        return self.raw.tensor_tensor_scan(out, d0, d1, initial, op0, op1)

    def recip(self, out, a):
        self._pre([a], [out])
        return self.raw.reciprocal(out, a)

    def act(self, out, a, func, bias=0.0, scale=1.0, accum_out=None):
        rd = [a] + ([bias] if isinstance(bias, bass.AP) else []) + ([scale] if isinstance(scale, bass.AP) else [])
        wr = [out] + ([accum_out] if accum_out is not None else [])
        self._pre(rd, wr)
        return self.raw.activation(out, a, func, bias=bias, scale=scale, accum_out=accum_out)

    def sign(self, out, a):
        self._pre([a], [out])
        return self.raw.sign(out, a)

    def iota(self, out, pattern, base=0, channel_multiplier=0):
        self._pre([], [out])
        return self.raw.iota(out, pattern, base=base, channel_multiplier=channel_multiplier,
                             allow_small_or_imprecise_dtypes=True)

    def affine_select(self, out, in_, compare_op, fill, base, pattern, channel_multiplier):
        self._pre([in_], [out])
        return self.raw.affine_select(out=out, in_=in_, compare_op=compare_op, fill=fill,
                                      base=base, pattern=pattern, channel_multiplier=channel_multiplier)

    def mm(self, out, lhsT, rhs, start, stop):
        self._pre([lhsT, rhs], [out])
        return self.raw.matmul(out, lhsT, rhs, start=start, stop=stop)

    def trans(self, out, in_, ident):
        self._pre([in_, ident], [out])
        return self.raw.transpose(out, in_, ident)

    def dma(self, out, in_, chan, transpose=False):
        if transpose:
            inst = self.raw.dma_start_transpose(out, in_)
        else:
            inst = self.raw.dma_start(out=out, in_=in_)
        chan.inc(inst)
        return chan.count


class Env:
    def __init__(self, nc, es):
        self.nc = nc
        self.es = es
        self.phase = 0
        self.psem = {e: es.enter_context(nc.semaphore(f"ph_{e}")) for e in ENGS}
        self._mk_engines()
        self.ydma = Chan(self, "ydma", step=16)
        self.gdma = Chan(self, "gdma", step=16)
        self.sdma = Chan(self, "sdma", step=16)
        self.vdma = Chan(self, "vdma", step=16)
        self.chans = {}
        self.sbuf_bytes = 0

    def _mk_engines(self):
        nc = self.nc
        self.t = None
        self.raw = dict(t=nc.tensor, v=nc.vector, s=nc.scalar, g=nc.gpsimd, y=nc.sync)
        self.w = {e: Eng(self, e, self.raw[e]) for e in ENGS}

    def chan(self, name, step=1):
        if name not in self.chans:
            self.chans[name] = Chan(self, name, step)
        return self.chans[name]

    def barrier(self):
        self.phase += 1
        k = self.phase
        for e in ENGS:
            w = self.w[e]
            if e != "y":
                w.drain()
            w.raw.sem_inc(self.psem[e], 1)
        for e in ENGS:
            w = self.w[e]
            for o in ENGS:
                if o != e:
                    w.raw.wait_ge(self.psem[o], k)
            w.written.clear()
            w.read.clear()
            w.n_ops = 0

    def barrier_small(self, tag):
        self.barrier()

    def sb(self, name, shape, dtype=F32):
        t = self.es.enter_context(self.nc.sbuf_tensor(name, list(shape), dtype))
        self.sbuf_bytes += shape[1] * mybir.dt.size(dtype) * 128
        return t

    def ps(self, name, shape, dtype=F32):
        return self.es.enter_context(self.nc.psum_tensor(name, list(shape), dtype))



def default_cfg():
    return dict(T=2048, C=1024, HF=4096, H=16, HD=64, LCH=256,
                LN_EPS=1e-5, NEG_INF=-1e38)



class PsumPool:

    def __init__(self, env, slices):
        self.env = env
        self.slots = [[ap, None, 0] for ap in slices]
        self.i = 0

    def acquire(self, eng):
        ap, chan, mark = self.slots[self.i % len(self.slots)]
        if chan is not None:
            chan.wait(eng, mark)
        self.idx_last = self.i % len(self.slots)
        self.i += 1
        return ap

    def release(self, chan, mark=None):
        s = self.slots[self.idx_last]
        s[1] = chan
        s[2] = chan.count if mark is None else mark


def bounce_row_to_bcast(env, row_ap, dst_tile, tag, width):
    nc = env.nc
    key = f"bnc_{tag}"
    dram = nc.dram_tensor(key, [1, width], F32, kind="Internal").ap()
    y = env.w["y"]
    y.dma(dram, row_ap, env.ydma)
    y.raw.wait_ge(env.ydma.sem, env.ydma.count)
    y.dma(dst_tile, dram.broadcast_to((128, width)), env.ydma)
    return env.ydma.count


class StatHelper:

    def __init__(self, env, cfg):
        self.env, self.cfg = env, cfg
        T = cfg["T"]
        self.ntt = T // 128
        self.tree = None
        self.ptr = None
        self.mm = env.chan("st_mm")
        self.ev = env.chan("st_ev")
        self.ident = None

    def tree_reduce(self, chunks, op, absolute=False, square_via=None):
        v = self.env.w["v"]
        s = self.env.w["s"]
        first = True
        for ap in chunks:
            src = ap
            if square_via is not None:
                s.act(square_via, ap, AF.Square)
                self.env.barrier()
                src = square_via
            if first:
                if absolute:
                    v.tt(self.tree[:], src, src, ALU.abs_max)
                else:
                    v.copy(self.tree[:], src)
                first = False
            else:
                if absolute:
                    v.tt(self.tree[:], self.tree[:], src, ALU.abs_max)
                else:
                    v.tt(self.tree[:], self.tree[:], src, ALU.add)

    def collapse(self, out_tmstat, op):
        env = self.env
        t, v = env.w["t"], env.w["v"]
        env.barrier()
        ev0 = self.ev.count
        mm0 = self.mm.count
        for b in range(self.ntt):
            slot = b % 2
            if b >= 2:
                self.ev.wait(t, ev0 + b - 1)
            inst = t.trans(self.ptr[:, slot * 512:slot * 512 + 128],
                           self.tree[:, b * 128:(b + 1) * 128], self.ident)
            self.mm.inc(inst)
            self.mm.wait(v, mm0 + b + 1)
            inst2 = v.red(out_tmstat[:, b:b + 1], self.ptr[:, slot * 512:slot * 512 + 128], op)
            self.ev.inc(inst2)
        env.barrier()

    def to_fm_bcast(self, tmstat_ap, dst_tile, tag):
        env = self.env
        nc = env.nc
        T = self.cfg["T"]
        dram = nc.dram_tensor(f"stb_{tag}", [T], F32, kind="Internal").ap()
        y = env.w["y"]
        y.dma(dram.rearrange("(c p) -> p c", p=128), tmstat_ap, env.ydma)
        y.raw.wait_ge(env.ydma.sem, env.ydma.count)
        y.dma(dst_tile, dram.rearrange("(t o) -> o t", o=1).broadcast_to((128, T)), env.ydma)
        return env.ydma.count



def emit_round(env, out_bf16, x_ap, tmpi, tmpa, tmpb):
    v = env.w["v"]
    g = env.w["g"]
    W = x_ap.shape[-1]
    for off in range(0, W, 128):
        v.copy(tmpi[:, 0:128], x_ap[:, off:off + 128])
        v.copy(tmpa[:, off:off + 128], tmpi[:, 0:128])
    v.tt(tmpb, x_ap, tmpa, ALU.subtract)
    v.stt(out_bf16, tmpb, 0.5, tmpa, ALU.is_ge, ALU.add)
    v.ts(tmpa, tmpb, -0.5, None, ALU.is_le)
    v.tt(out_bf16, out_bf16, tmpa, ALU.subtract)


def emit_digitize_q_sc8(env, out_bf16, x_ap, sc8b_ap, gs_col, tmp1, tmp2, tmp3,
                        rsc8b_ap=None):
    v, s = env.w["v"], env.w["s"]
    cv, cs = env.chan("c_v"), env.chan("c_s")
    v.tt(tmp1, x_ap, rsc8b_ap, ALU.mult)
    cv.inc_seq(v)
    cv.wait(s)
    s.act(tmp2, tmp1, AF.Abs, scale=gs_col)
    s.sign(tmp3, x_ap)
    cs.inc_seq(s)
    cs.wait(v)
    v.ts(tmp1, tmp2, 0.25, None, ALU.is_ge)
    v.stt(tmp1, tmp2, 0.75, tmp1, ALU.is_ge, ALU.add)
    v.stt(tmp1, tmp2, 1.25, tmp1, ALU.is_ge, ALU.add)
    v.stt(tmp1, tmp2, 1.75, tmp1, ALU.is_ge, ALU.add)
    v.ts(tmp1, tmp1, 0.5, None, ALU.mult)
    v.stt(tmp1, tmp2, 2.5, tmp1, ALU.is_ge, ALU.add)
    v.stt(tmp1, tmp2, 3.5, tmp1, ALU.is_ge, ALU.add)
    v.ts(tmp2, tmp2, 5.0, 2.0, ALU.is_ge, ALU.mult)
    v.tt(tmp1, tmp1, tmp2, ALU.add)
    v.tt(tmp1, tmp1, sc8b_ap, ALU.mult)
    v.tt(out_bf16, tmp1, tmp3, ALU.mult)


def emit_e4m3(env, out_ap, sc_ap, tmp1, tmp2):
    v, s = env.w["v"], env.w["s"]
    cv, cs = env.chan("c_v"), env.chan("c_s")
    LOG2E = 1.4426950408889634
    v.ts(tmp1, sc_ap, 1e-30, None, ALU.max)
    cv.inc_seq(v)
    cv.wait(s)
    s.act(tmp1, tmp1, AF.Ln)
    cs.inc_seq(s)
    cs.wait(v)
    v.ts(tmp1, tmp1, LOG2E, None, ALU.mult)
    v.ts(tmp2, tmp1, -5.0, None, ALU.is_ge)
    for j in range(-4, 9):
        v.stt(tmp2, tmp1, float(j), tmp2, ALU.is_ge, ALU.add)
    v.ts(tmp2, tmp2, -6.0 - 3.0, None, ALU.add)
    cv.inc_seq(v)
    cv.wait(s)
    LN2 = 0.6931471805599453
    s.act(tmp2, tmp2, AF.Exp, scale=LN2)
    cs.inc_seq(s)
    cs.wait(v)
    v.recip(tmp1, tmp2)
    v.tt(tmp1, sc_ap, tmp1, ALU.mult)
    v.ts(out_ap, tmp1, 0.5, None, ALU.is_ge)
    for j in range(1, 16):
        v.stt(out_ap, tmp1, j + 0.5, out_ap, ALU.is_ge, ALU.add)
    v.tt(out_ap, out_ap, tmp2, ALU.mult)



AW = 2064


def build_program(cfg=None):
    cfg = cfg or default_cfg()
    T, C, HF = cfg["T"], cfg["C"], cfg["HF"]
    H, HD, LCH = cfg["H"], cfg["HD"], cfg["LCH"]
    EPS = cfg["LN_EPS"]
    NCI = C // 128
    NHF = HF // 128
    NTT = T // 128
    NTS = T // 512
    NSL = T // LCH
    NPAIR = H // 2
    LQ = LCH
    NLQ = T // LQ
    nc = bass.Bass("TRN2", target_bir_lowering=False)

    def din(name, shape, dt=F32):
        return nc.dram_tensor(name, list(shape), dt, kind="ExternalInput").ap()

    xT = din("xT", [C, T])
    wkT = din("wkT", [C, C]); wvT = din("wvT", [C, C]); wrT = din("wrT", [C, C])
    wckT = din("wckT", [C, HF]); wcvT = din("wcvT", [HF, C]); wcrT = din("wcrT", [C, C])
    vecs = din("vecs", [12, C])
    w1T = din("w1T", [C, HF]); w2T = din("w2T", [HF, C])
    convw = din("convw", [HF, 3]); convb = din("convb", [HF, 1])
    wqkv = din("wqkv", [C, 3 * C])
    wkv2 = din("wkv2", [C, 2 * C])
    wout = din("wout", [C, C])
    boutv = din("boutv", [1, C])
    gw = din("gw", [HF, C]); uw = din("uw", [HF, C]); dw = din("dw", [C, HF])
    sinv = din("sinv", [1, 2])
    rotf = din("rotf", [1, HD // 2])

    outT = nc.dram_tensor("outT", [C, T], F32, kind="ExternalOutput").ap()

    def dsc(name, shape, dt=F32):
        return nc.dram_tensor(name, list(shape), dt, kind="Internal").ap()

    h_dram = dsc("h_dram", [NHF, 128, T])
    g_dram = dsc("g_dram", [NCI, 128, T])
    x1_dram = dsc("x1_dram", [NCI, 128, T])
    g1_dram = dsc("g1_dram", [NCI, 128, T])
    y_dram = dsc("y_dram", [H * HD, T])
    w1q_dram = dsc("w1q_dram", [NCI, 128, HF], BF16)
    w2q_dram = dsc("w2q_dram", [NHF, 128, C], BF16)
    gq_dram = dsc("gq_dram", [HF, C], BF16)
    uq_dram = dsc("uq_dram", [HF, C], BF16)
    dq_dram = dsc("dq_dram", [C, HF], BF16)

    dbg = cfg.get("debug", False)
    STOP = cfg.get("stop_stage", 99)
    if dbg:
        dbg_xq = dsc("dbg_xq", [NCI, 128, T], BF16)
        dbg_ep = dsc("dbg_ep", [128, T])
        dbg_scp = dsc("dbg_scp", [128, T])
        dbg_scq = dsc("dbg_scq", [128, T])
        dbg_p0 = dsc("dbg_p0", [128, T])
        dbg_q0 = dsc("dbg_q0", [128, T])
        dbg_u = dsc("dbg_u", [128, 512])
        dbg_hb = dsc("dbg_hb", [128, 2050])
        dbg_xk = dsc("dbg_xk", [128, 2048])
        dbg_kk = dsc("dbg_kk", [128, 256])
        dbg_vv = dsc("dbg_vv", [128, 256])
        dbg_sr = dsc("dbg_sr", [128, 256])
        dbg_y0 = dsc("dbg_y0", [128, 256])
        dbg_x0 = dsc("dbg_x0", [128, 512])
        dbg_scp2 = dsc("dbg_scp2", [128, 512])
        dbg_scq2 = dsc("dbg_scq2", [128, 512])
    es = contextlib.ExitStack()
    with es:
        es.enter_context(nc.allow_non_contiguous_dma(reason="small vector/tile staging DMAs"))
        env = Env(nc, es)
        t, v, s, g, y = (env.w[e] for e in ENGS)

        pbig = env.ps("pbig", [128, 4096])
        x = [env.sb(f"x{i}", [128, T]) for i in range(NCI)]
        bc = [env.sb(f"bc{i}", [128, T]) for i in range(4)]
        A = [env.sb(f"A{i}", [128, 2080 if i == 3 else 2056]) for i in range(8)]
        B = [env.sb(f"B{i}", [128, T], BF16) for i in range(8)]
        tI = env.sb("tI", [128, 128], I32)
        cols = env.sb("cols", [128, 104], F32)
        colsS = env.sb("colsS", [128, 64], F32)
        tmst = env.sb("tmst", [128, 160], F32)
        identf = env.sb("identf", [128, 128], F32)
        ones_col = env.sb("ones_col", [128, 1], F32)
        wq_small = [env.sb(f"wqs{i}", [128, 1024], BF16) for i in range(2)]
        smallf = env.sb("smallf", [128, 1920], F32)
        boutc = env.sb("boutc", [128, NCI], F32)
        convw_sb = bc[3][:, 0:NHF * 3]
        convb_sb = bc[3][:, NHF * 3:NHF * 4]
        sct = smallf_sct_placeholder = None
        kktv = smallf[:, 1664:1920]
        sqb = smallf[:, 1024:1280]
        sct = smallf[:, 1280:1344]
        onesr = smallf[0:1, 1538:1602]
        onesr64 = smallf[64:65, 1538:1602]
        sgrT = kktv[:, 0:256]

        st = StatHelper(env, cfg)
        st.ptr = pbig[:, 3072:4096]
        st.ident = identf[:]
        st.tree = A[3]

        tA, tB, tC = A[0], A[1], A[2]

        def bank(b, w=512):
            return pbig[:, b * 512:b * 512 + w]

        pmm = PsumPool(env, [bank(4), bank(5)])
        pst2 = bank(6)

        mmc = env.chan("mmc")
        evc = env.chan("evc")
        cv = env.chan("c_v"); cs_ = env.chan("c_s"); cg = env.chan("c_g")
        wdma = env.ydma
        gdma = env.gdma

        def dwait(eng=None):
            y.raw.wait_ge(wdma.sem, wdma.count)
            if eng is not None:
                wdma.wait(eng)

        zbuf = smallf[:, 0:257]
        vbuf2 = smallf[:, 257:514]
        wk1 = smallf[:, 514:770]
        wk2 = smallf[:, 770:1026]
        wk3 = smallf[:, 1026:1282]
        wk4 = smallf[:, 1282:1538]


        g.memset(identf[:], 0.0)
        g.drain()
        g.affine_select(identf[:], identf[:], ALU.not_equal, 1.0, 0, [[-1, 128]], 1)
        g.memset(cols[:], 0.0)
        g.memset(colsS[:], 0.0)
        v.memset(ones_col[:], 1.0)
        env.barrier()
        for ci in range(NCI):
            y.dma(x[ci][:], xT[ci * 128:(ci + 1) * 128, :], wdma)
        for vi in range(11):
            y.dma(cols[:, vi * NCI:(vi + 1) * NCI],
                  vecs[vi, :].rearrange("(ci p) -> p ci", p=128), wdma)
        y.dma(boutc[:], boutv[0].rearrange("(ci p) -> p ci", p=128), wdma)
        dwait()
        env.barrier()
        v.tt(cols[:, 11 * NCI:12 * NCI], cols[:, 0:NCI], cols[:, NCI:2 * NCI], ALU.max)
        env.barrier()

        def col(vi, ci):
            return cols[:, vi * NCI + ci:vi * NCI + ci + 1]

        def tms(i):
            return tmst[:, i * 32:i * 32 + NTT]

        def ln_stats(produce_chunk, p_out, q_out):
            for ci in range(NCI):
                produce_chunk(ci, tB[:, 0:T])
                cv.inc_seq(v)
                cv.wait(s)
                s.act(tC[:, 0:T], tB[:, 0:T], AF.Square)
                if ci == 0:
                    v.copy(st.tree[:, 0:T], tB[:, 0:T])
                else:
                    v.tt(st.tree[:, 0:T], st.tree[:, 0:T], tB[:, 0:T], ALU.add)
                cs_.inc_seq(s)
                cs_.wait(v)
                if ci == 0:
                    v.copy(tA[:, 0:T], tC[:, 0:T])
                else:
                    v.tt(tA[:, 0:T], tA[:, 0:T], tC[:, 0:T], ALU.add)
            st.collapse(tms(0), ALU.add)
            v.copy(st.tree[:, 0:T], tA[:, 0:T])
            st.collapse(tms(1), ALU.add)
            v.ts(tms(0), tms(0), 1.0 / C, None, ALU.mult)
            v.ts(tms(1), tms(1), 1.0 / C, None, ALU.mult)
            v.copy(tmst[:, 144:144 + NTT], tms(0))
            v.tt(tmst[:, 128:128 + NTT], tms(0), tms(0), ALU.mult)
            v.tt(tms(1), tms(1), tmst[:, 128:128 + NTT], ALU.subtract)
            v.ts(tms(1), tms(1), EPS, None, ALU.add)
            cv.inc_seq(v)
            cv.wait(s)
            s.act(tms(1), tms(1), AF.Ln)
            cs_.inc_seq(s)
            cs_.wait(v)
            v.ts(tms(1), tms(1), -0.5, None, ALU.mult)
            cv.inc_seq(v)
            cv.wait(s)
            s.act(p_out, tms(1), AF.Exp)
            cs_.inc_seq(s)
            cs_.wait(v)
            v.tt(q_out, tmst[:, 144:144 + NTT], p_out, ALU.mult)
            env.barrier()

        def bounce1(tm_ap, dst_tile, tag):
            st.to_fm_bcast(tm_ap, dst_tile[:, 0:T], tag)
            y.raw.wait_ge(env.ydma.sem, env.ydma.count)
            env.barrier()

        def bounce_scalar_to_col(src_ap, dst_col, tag):
            d = nc.dram_tensor(f"bsc_{tag}", [1, 1], F32, kind="Internal").ap()
            y.dma(d, src_ap, wdma)
            dwait()
            y.dma(dst_col, d.broadcast_to((128, 1)), wdma)
            dwait()
            env.barrier()

        def finish_early0():
            for cs_i in range(NCI):
                y.dma(outT[cs_i * 128:(cs_i + 1) * 128, :], x[cs_i][:, 0:T], wdma)
            dwait()
            env.barrier()

        if STOP <= 0:
            finish_early0()
            return nc

        env.barrier()

        def prod_x(ci, dst):
            v.copy(dst, x[ci][:, 0:T])

        ln_stats(prod_x, tms(2), tms(3))
        P0B, Q0B = bc[0], bc[1]
        bounce1(tms(2), P0B, "p0")
        bounce1(tms(3), Q0B, "q0")

        def xn0_chunk(ci, dst):
            v.tt(dst, x[ci][:, 0:T], P0B[:, 0:T], ALU.mult)
            v.tt(dst, dst, Q0B[:, 0:T], ALU.subtract)

        if STOP <= 1:
            finish_early0()
            return nc

        for ci in range(NCI):
            xn0_chunk(ci, tB[:, 0:T])
            cv.inc_seq(v)
            cv.wait(s)
            s.act(tA[:, 0:T], tB[:, 0:T], AF.Abs)
            cs_.inc_seq(s)
            cs_.wait(v)
            if ci == 0:
                v.copy(st.tree[:, 0:T], tA[:, 0:T])
            else:
                v.tt(st.tree[:, 0:T], st.tree[:, 0:T], tA[:, 0:T], ALU.max)
        st.collapse(tms(0), ALU.max)
        v.ts(tms(0), tms(0), 1e-5, None, ALU.max)
        v.recip(tms(1), tms(0))
        v.ts(tms(1), tms(1), 127.0, None, ALU.mult)
        v.tt(tmst[:, 128:128 + NTT], tms(2), tms(1), ALU.mult)
        v.tt(tmst[:, 144:144 + NTT], tms(3), tms(1), ALU.mult)
        v.ts(tms(0), tms(0), 1.0 / 127.0, None, ALU.mult)
        env.barrier()
        SCPB, SCQB = bc[2], bc[3]
        bounce1(tmst[:, 128:128 + NTT], SCPB, "scp")
        bounce1(tmst[:, 144:144 + NTT], SCQB, "scq")

        if dbg:
            y.dma(dbg_p0, P0B[:, 0:T], wdma)
            y.dma(dbg_q0, Q0B[:, 0:T], wdma)
            y.dma(dbg_scp, SCPB[:, 0:T], wdma)
            y.dma(dbg_scq, SCQB[:, 0:T], wdma)
            dwait()
        xq = B
        for ci in range(NCI):
            for ts in range(NTS):
                sl = slice(ts * 512, (ts + 1) * 512)
                v.tt(tB[:, sl], x[ci][:, sl], SCPB[:, sl], ALU.mult)
                v.tt(tB[:, sl], tB[:, sl], SCQB[:, sl], ALU.subtract)
                if dbg and ci == 0 and ts == 0:
                    cv.inc_seq(v)
                    cv.wait(y)
                    y.dma(dbg_u, tB[:, sl], wdma)
                    y.dma(dbg_x0, x[0][:, sl], wdma)
                    y.dma(dbg_scp2, SCPB[:, sl], wdma)
                    y.dma(dbg_scq2, SCQB[:, sl], wdma)
                    dwait(v)
                emit_round(env, xq[ci][:, sl], tB[:, sl], tI, tA[:, sl], tC[:, sl])
        env.barrier()
        if dbg:
            for ci in range(NCI):
                y.dma(dbg_xq[ci], xq[ci][:, 0:T], wdma)
            dwait()
            env.barrier()

        if STOP <= 2.1:
            finish_early0()
            return nc

        def weight_mean_pass(wT_dram, nchunks):
            Fw = wT_dram.shape[1]
            half = Fw // 2
            for ci in range(nchunks):
                for hh in range(2):
                    bi = (2 * ci + hh) % 2
                    env.barrier()
                    y.dma(A[4 + bi][:, 0:half], wT_dram[ci * 128:(ci + 1) * 128, hh * half:(hh + 1) * half], wdma)
                    dwait()
                    env.barrier()
                    v.red(colsS[:, 20:21], A[4 + bi][:, 0:half], ALU.add, absolute=True)
                    if ci == 0 and hh == 0:
                        v.copy(colsS[:, 18:19], colsS[:, 20:21])
                    else:
                        v.tt(colsS[:, 18:19], colsS[:, 18:19], colsS[:, 20:21], ALU.add)
            env.barrier()
            inst = t.mm(pst2[0:1, 0:1], colsS[:, 18:19], ones_col[:], True, True)
            mmc.inc(inst)
            mmc.wait(v)
            v.copy(colsS[0:1, 21:22], pst2[0:1, 0:1])
            env.barrier()

        def weight_tern_pass(wT_dram, q_dram, nchunks):
            Fw = wT_dram.shape[1]
            half = Fw // 2
            for ci in range(nchunks):
                for hh in range(2):
                    bi = (2 * ci + hh) % 2
                    wbv = A[4 + bi][:, 0:half]
                    env.barrier()
                    y.dma(wbv, wT_dram[ci * 128:(ci + 1) * 128, hh * half:(hh + 1) * half], wdma)
                    dwait()
                    env.barrier()
                    v.ts(tA[:, 0:half], wbv, colsS[:, 0:1], None, ALU.is_gt)
                    v.ts(tB[:, 0:half], wbv, colsS[:, 2:3], None, ALU.is_lt)
                    qw = min(1024, half)
                    for qt in range(half // qw):
                        qsl = slice(qt * qw, (qt + 1) * qw)
                        v.tt(wq_small[qt % 2][:, 0:qw], tA[:, qsl], tB[:, qsl], ALU.subtract)
                        cv.inc_seq(v)
                        cv.wait(y)
                        y.dma(q_dram[ci][:, hh * half + qt * qw:hh * half + (qt + 1) * qw],
                              wq_small[qt % 2][:, 0:qw], wdma)
                        dwait(v)
            dwait()
            env.barrier()

        weight_mean_pass(w1T, NCI)
        v.ts(colsS[0:1, 21:22], colsS[0:1, 21:22], 1.0 / (C * HF), None, ALU.mult)
        v.ts(colsS[0:1, 21:22], colsS[0:1, 21:22], 1e-5, None, ALU.max)
        v.ts(colsS[0:1, 22:23], colsS[0:1, 21:22], 0.5, None, ALU.mult)
        env.barrier()
        bounce_scalar_to_col(colsS[0:1, 22:23], colsS[:, 0:1], "thr1")
        bounce_scalar_to_col(colsS[0:1, 21:22], colsS[:, 1:2], "swi1")
        v.ts(colsS[:, 2:3], colsS[:, 0:1], -1.0, None, ALU.mult)
        v.ts(tmst[:, 128:128 + NTT], tms(0), colsS[:, 1:2], None, ALU.mult)
        env.barrier()
        EPB = bc[2]
        bounce1(tmst[:, 128:128 + NTT], EPB, "ep1")
        weight_tern_pass(w1T, w1q_dram, NCI)
        if dbg:
            y.dma(dbg_ep, EPB[:, 0:T], wdma)
            dwait()
            env.barrier()

        if STOP <= 2.2:
            finish_early0()
            return nc

        y.dma(convw_sb[:].rearrange("p (hfs k) -> p hfs k", k=3), convw.rearrange("(hfs p) k -> p hfs k", p=128), wdma)
        y.dma(convb_sb[:], convb[:, 0].rearrange("(hfs p) -> p hfs", p=128), wdma)
        dwait()
        env.barrier()
        first_tree = True
        for hfs in range(NHF):
            bi = hfs % 2
            mmc.wait(y)
            for ci in range(NCI):
                y.dma(wq_small[bi][:, ci * 128:(ci + 1) * 128],
                      w1q_dram[ci][:, hfs * 128:(hfs + 1) * 128], wdma)
            dwait(t)
            hb = A[4 + bi]
            for ts in range(NTS):
                ps = pmm.acquire(t)
                for ci in range(NCI):
                    inst = t.mm(ps, wq_small[bi][:, ci * 128:(ci + 1) * 128],
                                xq[ci][:, ts * 512:(ts + 1) * 512], ci == 0, ci == NCI - 1)
                mmc.inc(inst)
                mmc.wait(v)
                v.tt(hb[:, 1 + ts * 512:1 + (ts + 1) * 512], ps, EPB[:, ts * 512:(ts + 1) * 512], ALU.mult)
                evc.inc_seq(v)
                pmm.release(evc)
            v.memset(hb[:, 0:1], 0.0)
            v.memset(hb[:, T + 1:T + 2], 0.0)
            if dbg and hfs == 0:
                cv.inc_seq(v)
                cv.wait(y)
                y.dma(dbg_hb, hb[:, 0:2050], wdma)
                dwait(v)
            v.ts(tA[:, 0:T], hb[:, 0:T], convw_sb[:, hfs * 3:hfs * 3 + 1], None, ALU.mult)
            v.stt(tA[:, 0:T], hb[:, 1:T + 1], convw_sb[:, hfs * 3 + 1:hfs * 3 + 2], tA[:, 0:T], ALU.mult, ALU.add)
            v.stt(tA[:, 0:T], hb[:, 2:T + 2], convw_sb[:, hfs * 3 + 2:hfs * 3 + 3], tA[:, 0:T], ALU.mult, ALU.add)
            cv.inc_seq(v)
            cv.wait(s)
            s.act(tB[:, 0:T], tA[:, 0:T], AF.Silu, bias=convb_sb[:, hfs:hfs + 1])
            cs_.inc_seq(s)
            cs_.wait(v)
            cs_.wait(y)
            y.dma(h_dram[hfs], tB[:, 0:T], wdma)
            s.act(tC[:, 0:T], tB[:, 0:T], AF.Abs)
            cs_.inc_seq(s)
            cs_.wait(v)
            if first_tree:
                v.copy(st.tree[:, 0:T], tC[:, 0:T])
                first_tree = False
            else:
                v.tt(st.tree[:, 0:T], st.tree[:, 0:T], tC[:, 0:T], ALU.max)
            dwait()
            env.barrier()

        if STOP <= 2.3:
            finish_early0()
            return nc

        st.collapse(tms(0), ALU.max)
        v.ts(tms(0), tms(0), 1e-5, None, ALU.max)
        v.recip(tms(1), tms(0))
        v.ts(tms(1), tms(1), 127.0, None, ALU.mult)
        v.ts(tms(0), tms(0), 1.0 / 127.0, None, ALU.mult)
        env.barrier()
        HSCB = bc[2]
        bounce1(tms(1), HSCB, "hsc")

        weight_mean_pass(w2T, NHF)
        v.ts(colsS[0:1, 21:22], colsS[0:1, 21:22], 1.0 / (C * HF), None, ALU.mult)
        v.ts(colsS[0:1, 21:22], colsS[0:1, 21:22], 1e-5, None, ALU.max)
        v.ts(colsS[0:1, 22:23], colsS[0:1, 21:22], 0.5, None, ALU.mult)
        env.barrier()
        bounce_scalar_to_col(colsS[0:1, 22:23], colsS[:, 0:1], "thr2")
        bounce_scalar_to_col(colsS[0:1, 21:22], colsS[:, 1:2], "swi2")
        v.ts(colsS[:, 2:3], colsS[:, 0:1], -1.0, None, ALU.mult)
        v.ts(tmst[:, 128:128 + NTT], tms(0), colsS[:, 1:2], None, ALU.mult)
        env.barrier()
        EP2B = bc[3]
        bounce1(tmst[:, 128:128 + NTT], EP2B, "ep2")
        weight_tern_pass(w2T, w2q_dram, NHF)

        if STOP <= 2.4:
            finish_early0()
            return nc

        hq = [B[i // 4][:, (i % 4) * 512:(i % 4 + 1) * 512] for i in range(NHF)]
        for ts in range(NTS):
            sl = slice(ts * 512, (ts + 1) * 512)
            for hfs in range(NHF):
                bi = hfs % 2
                y.dma(A[4 + bi][:, 0:512], h_dram[hfs][:, sl], wdma)
                dwait()
                env.barrier()
                v.tt(tB[:, 0:512], A[4 + bi][:, 0:512], HSCB[:, sl], ALU.mult)
                emit_round(env, hq[hfs][:, 0:512], tB[:, 0:512], tI, tA[:, 0:512], tC[:, 0:512])
            env.barrier()
            mgc = env.chan("mgc")
            for cs in range(NCI):
                ps = pmm.acquire(t)
                ngrp = NHF // 8
                for grp in range(ngrp):
                    bi = grp % 2
                    if mgc.count >= 1:
                        mgc.wait(y, mgc.count - 1)
                    for j in range(8):
                        hfs = grp * 8 + j
                        y.dma(wq_small[bi][:, j * 128:(j + 1) * 128],
                              w2q_dram[hfs][:, cs * 128:(cs + 1) * 128], wdma)
                    dwait(t)
                    for j in range(8):
                        hfs = grp * 8 + j
                        inst = t.mm(ps, wq_small[bi][:, j * 128:(j + 1) * 128], hq[hfs][:, 0:512],
                                    hfs == 0, hfs == NHF - 1)
                    mgc.inc(inst)
                mgc.wait(v)
                v.tt(tA[:, 0:512], ps, EP2B[:, sl], ALU.mult)
                evc.inc_seq(v)
                pmm.release(evc)
                cv.inc_seq(v)
                cv.wait(s)
                s.act(tB[:, 0:512], tA[:, 0:512], AF.Silu)
                cs_.inc_seq(s)
                cs_.wait(y)
                y.dma(g_dram[cs][:, sl], tB[:, 0:512], wdma)
                dwait()
                env.barrier()

        def finish_early():
            for cs_i in range(NCI):
                y.dma(outT[cs_i * 128:(cs_i + 1) * 128, :], x[cs_i][:, 0:T], wdma)
            dwait()
            env.barrier()

        if STOP <= 2:
            finish_early()
            return nc

        ln_stats(xn0_chunk, tms(0), tms(1))
        v.tt(tmst[:, 128:128 + NTT], tms(2), tms(0), ALU.mult)
        v.tt(tmst[:, 144:144 + NTT], tms(3), tms(0), ALU.mult)
        v.tt(tmst[:, 144:144 + NTT], tmst[:, 144:144 + NTT], tms(1), ALU.add)
        env.barrier()
        P1B, Q1B = bc[2], bc[3]
        bounce1(tmst[:, 128:128 + NTT], P1B, "p1")
        bounce1(tmst[:, 144:144 + NTT], Q1B, "q1")

        wchd = env.sb("wchd", [128, 384], F32)
        for dso in range(NCI):
            v.memset(colsS[:, 24 + dso:25 + dso], 0.0)
            v.memset(colsS[:, 32 + dso:33 + dso], -1e38)
        xabuf, xks, xvs, xrs = A[4], A[5], A[6], A[7]
        for ci in range(NCI):
            v.memset(xabuf[:, ci * (LCH + 1):ci * (LCH + 1) + 1], 0.0)
        env.barrier()

        mmk = env.chan("mmk")
        prjc = env.chan("prjc")
        wkvv = env.chan("wkvv")
        wkvs = env.chan("wkvs")
        set_marks = [(0, 0), (0, 0)]

        for ts in range(NSL):
            t0 = ts * LCH
            sl = slice(t0, t0 + LCH)
            for ci in range(NCI):
                o = ci * (LCH + 1)
                xa = xabuf[:, o + 1:o + 1 + LCH]
                xp = xabuf[:, o:o + LCH]
                v.tt(xa, x[ci][:, sl], P1B[:, sl], ALU.mult)
                v.tt(xa, xa, Q1B[:, sl], ALU.subtract)
                v.ts(xa, xa, col(7, ci), col(8, ci), ALU.mult, ALU.add)
                dseg = tA[:, ci * LCH:(ci + 1) * LCH]
                v.tt(dseg, xa, xp, ALU.subtract)
                v.stt(xks[:, ci * LCH:(ci + 1) * LCH], dseg, col(2, ci), xp, ALU.mult, ALU.add)
                v.stt(xvs[:, ci * LCH:(ci + 1) * LCH], dseg, col(3, ci), xp, ALU.mult, ALU.add)
                v.stt(xrs[:, ci * LCH:(ci + 1) * LCH], dseg, col(4, ci), xp, ALU.mult, ALU.add)
            env.barrier()
            if dbg and ts == 0:
                y.dma(dbg_xk, xks[:, 0:2048], wdma)
                dwait()
                env.barrier()
            for dso in range(NCI):
                sset = dso % 2
                pk = bank(sset * 3 + 0, LCH)
                pv = bank(sset * 3 + 1, LCH)
                pr = bank(sset * 3 + 2, LCH)
                wkvv.wait(t, set_marks[sset][0])
                wkvs.wait(t, set_marks[sset][1])
                for ci in range(NCI):
                    bi = ci % 2
                    prjc.wait(y)
                    y.dma(wchd[:, 0 * 384:0 * 384 + 128],
                          wkT[ci * 128:(ci + 1) * 128, dso * 128:(dso + 1) * 128], wdma)
                    y.dma(wchd[:, 0 * 384 + 128:0 * 384 + 256],
                          wvT[ci * 128:(ci + 1) * 128, dso * 128:(dso + 1) * 128], wdma)
                    y.dma(wchd[:, 0 * 384 + 256:0 * 384 + 384],
                          wrT[ci * 128:(ci + 1) * 128, dso * 128:(dso + 1) * 128], wdma)
                    dwait(t)
                    cseg = slice(ci * LCH, (ci + 1) * LCH)
                    t.mm(pk, wchd[:, 0 * 384:0 * 384 + 128], xks[:, cseg], ci == 0, ci == NCI - 1)
                    t.mm(pv, wchd[:, 0 * 384 + 128:0 * 384 + 256], xvs[:, cseg], ci == 0, ci == NCI - 1)
                    inst = t.mm(pr, wchd[:, 0 * 384 + 256:0 * 384 + 384], xrs[:, cseg], ci == 0, ci == NCI - 1)
                    prjc.inc(inst)
                prjc.wait(v)
                prjc.wait(s)
                wdma.wait(v)
                if dbg and ts == 0 and dso == 0:
                    v.copy(tA[:, 0:LCH], pk)
                    v.copy(tA[:, LCH:2 * LCH], pv)
                    cv.inc_seq(v)
                    cv.wait(y)
                    y.dma(dbg_kk, tA[:, 0:LCH], wdma)
                    y.dma(dbg_vv, tA[:, LCH:2 * LCH], wdma)
                    dwait(v)
                aac = colsS[:, 24 + dso:25 + dso]
                ppc = colsS[:, 32 + dso:33 + dso]
                wc = col(0, dso); uc = col(1, dso)
                mwc = cols[:, 11 * NCI + dso:11 * NCI + dso + 1]
                Mc = colsS[:, 40:41]
                bw = colsS[:, 41:42]; bu = colsS[:, 42:43]; bp = colsS[:, 43:44]
                v.red(Mc, pk, ALU.max)
                v.tt(Mc, Mc, mwc, ALU.add)
                v.tt(Mc, Mc, ppc, ALU.max)
                v.tt(bw, wc, Mc, ALU.subtract)
                v.tt(bu, uc, Mc, ALU.subtract)
                v.tt(bp, ppc, Mc, ALU.subtract)
                bm = colsS[:, 54:55]
                v.ts(bm, Mc, -1.0, None, ALU.mult)
                cv.inc_seq(v)
                cv.wait(s)
                s.act(wk1, pk, AF.Exp, bias=bw)
                s.act(wk2, pk, AF.Exp, bias=bu)
                s.act(A[1][:, 0:LCH], pk, AF.Exp, bias=bm)
                s.act(colsS[:, 44:45], bp, AF.Exp)
                s.act(colsS[:, 45:46], wc, AF.Exp)
                s.act(sgrT, pr, AF.Sigmoid)
                cs_.inc_seq(s)
                wkvs.inc_seq(s)
                cs_.wait(v)
                v.copy(zbuf[:, 0:1], colsS[:, 44:45])
                v.scan(zbuf[:, 1:LCH + 1], colsS[:, 45:46].broadcast_to((128, LCH)), wk1,
                       colsS[:, 44:45], ALU.mult, ALU.add)
                v.tt(wk3, zbuf[:, 0:LCH], A[1][:, 0:LCH], ALU.add)
                v.tt(wk4, zbuf[:, 0:LCH], A[1][:, 0:LCH], ALU.max)
                v.recip(wk4, wk4)
                v.tt(wk3, wk3, wk4, ALU.mult)
                v.ts(wk4, wk3, colsS[:, 45:46], None, ALU.mult)
                v.tt(wk3, wk4, A[1][:, 0:LCH], ALU.mult)
                v.tt(wk3, wk3, pv, ALU.mult)
                v.tt(colsS[:, 46:47], aac, colsS[:, 44:45], ALU.mult)
                v.copy(vbuf2[:, 0:1], colsS[:, 46:47])
                v.scan(vbuf2[:, 1:LCH + 1], wk4, wk3, colsS[:, 46:47], ALU.mult, ALU.add)
                v.tt(wk3, wk2, pv, ALU.mult)
                v.tt(wk3, wk3, vbuf2[:, 0:LCH], ALU.add)
                v.tt(wk4, zbuf[:, 0:LCH], wk2, ALU.add)
                v.recip(wk4, wk4)
                v.tt(wk3, wk3, wk4, ALU.mult)
                if dbg and ts == 0 and dso == 0:
                    cv.inc_seq(v)
                    cv.wait(y)
                    y.dma(dbg_y0, wk3, wdma)
                    y.dma(dbg_sr, sgrT, wdma)
                    dwait(v)
                v.tt(wk4, x[dso][:, sl], P0B[:, sl], ALU.mult)
                v.tt(wk4, wk4, Q0B[:, sl], ALU.subtract)
                v.tt(wk3, wk3, sgrT, ALU.mult)
                v.tt(wk3, wk3, wk4, ALU.add)
                v.recip(colsS[:, 55:56], zbuf[:, LCH:LCH + 1])
                v.tt(aac, vbuf2[:, LCH:LCH + 1], colsS[:, 55:56], ALU.mult)
                cv.inc_seq(v)
                cv.wait(s)
                s.act(colsS[:, 47:48], zbuf[:, LCH:LCH + 1], AF.Ln)
                cs_.inc_seq(s)
                cs_.wait(v)
                v.tt(ppc, Mc, colsS[:, 47:48], ALU.add)
                cv.inc_seq(v)
                wkvv.inc_seq(v)
                set_marks[sset] = (wkvv.count, wkvs.count)
                cv.wait(y)
                y.dma(x1_dram[dso][:, sl], wk3, wdma)
            dwait()
            for ci in range(NCI):
                o = ci * (LCH + 1)
                v.copy(xabuf[:, o:o + 1], xabuf[:, o + LCH:o + LCH + 1])
            env.barrier()

        def prod_x1(ci, dst):
            env.barrier()
            y.dma(dst, x1_dram[ci], wdma)
            dwait()
            env.barrier()

        ln_stats(prod_x1, tms(0), tms(1))
        P3B, Q3B = bc[0], bc[1]
        bounce1(tms(0), P3B, "p3")
        bounce1(tms(1), Q3B, "q3")

        x1s, xbbuf, xckr, vcacc = A[4], A[5], A[6], A[7]
        kcg = A[3]
        wcvb = A[0]
        for ci in range(NCI):
            v.memset(xbbuf[:, ci * (LCH + 1):ci * (LCH + 1) + 1], 0.0)
        env.barrier()

        kcc = env.chan("kcc")
        vca = env.chan("vca")
        wcv_pf = env.chan("wcv_pf")

        for ts in range(NSL):
            t0 = ts * LCH
            sl = slice(t0, t0 + LCH)
            for ci in range(NCI):
                y.dma(x1s[:, ci * LCH:(ci + 1) * LCH], x1_dram[ci][:, sl], wdma)
            dwait()
            env.barrier()
            for ci in range(NCI):
                o = ci * (LCH + 1)
                xb = xbbuf[:, o + 1:o + 1 + LCH]
                xp = xbbuf[:, o:o + LCH]
                v.tt(xb, x1s[:, ci * LCH:(ci + 1) * LCH], P3B[:, sl], ALU.mult)
                v.tt(xb, xb, Q3B[:, sl], ALU.subtract)
                v.ts(xb, xb, col(9, ci), col(10, ci), ALU.mult, ALU.add)
                dseg = tC[:, ci * LCH:(ci + 1) * LCH]
                v.tt(dseg, xb, xp, ALU.subtract)
                v.stt(xckr[:, ci * LCH:(ci + 1) * LCH], dseg, col(5, ci), xp, ALU.mult, ALU.add)
            env.barrier()
            for hp2 in range(NHF // 8):
                for rnd in range(2):
                    for j in range(4):
                        hfs = hp2 * 8 + rnd * 4 + j
                        pkc = bank(j, LCH)
                        kcc.wait(t)
                        for ci in range(NCI):
                            bi = ci % 2
                            prjc.wait(y)
                            y.dma(wchd[:, 0 * 384:0 * 384 + 128],
                                  wckT[ci * 128:(ci + 1) * 128, hfs * 128:(hfs + 1) * 128], wdma)
                            dwait(t)
                            inst = t.mm(pkc, wchd[:, 0 * 384:0 * 384 + 128],
                                        xckr[:, ci * LCH:(ci + 1) * LCH], ci == 0, ci == NCI - 1)
                            prjc.inc(inst)
                        prjc.wait(s)
                        kslot = kcg[:, (rnd * 4 + j) * LCH:(rnd * 4 + j + 1) * LCH]
                        s.act(tB[:, 0:LCH], pkc, AF.Relu)
                        s.act(kslot, tB[:, 0:LCH], AF.Square)
                        kcc.inc_seq(s)
                for cs in range(NCI):
                    pvc = bank(4 + (cs % 2), LCH)
                    vca.wait(t, max(0, vca.count - 1))
                    kcc.wait(t)
                    for jj in range(8):
                        hfs = hp2 * 8 + jj
                        bi = jj % 2
                        wcv_pf.wait(y, max(0, wcv_pf.count - 1))
                        y.dma(wcvb[:, bi * 1024 + 0:bi * 1024 + 128],
                              wcvT[hfs * 128:(hfs + 1) * 128, cs * 128:(cs + 1) * 128], wdma)
                        dwait(t)
                        inst = t.mm(pvc, wcvb[:, bi * 1024:bi * 1024 + 128],
                                    kcg[:, jj * LCH:(jj + 1) * LCH], jj == 0, jj == 7)
                        wcv_pf.inc(inst)
                    wcv_pf.wait(v)
                    vseg = vcacc[:, cs * LCH:(cs + 1) * LCH]
                    if hp2 == 0:
                        inst2 = v.copy(vseg, pvc)
                    else:
                        inst2 = v.tt(vseg, vseg, pvc, ALU.add)
                    vca.inc(inst2)
            env.barrier()
            for ci in range(NCI):
                o = ci * (LCH + 1)
                xb = xbbuf[:, o + 1:o + 1 + LCH]
                xp = xbbuf[:, o:o + LCH]
                v.stt(xckr[:, ci * LCH:(ci + 1) * LCH], tC[:, ci * LCH:(ci + 1) * LCH],
                      col(6, ci), xp, ALU.mult, ALU.add)
            env.barrier()
            for cs in range(NCI):
                pscr = bank(6, LCH)
                for ci in range(NCI):
                    bi = ci % 2
                    prjc.wait(y)
                    y.dma(wchd[:, 0 * 384:0 * 384 + 128],
                          wcrT[ci * 128:(ci + 1) * 128, cs * 128:(cs + 1) * 128], wdma)
                    dwait(t)
                    inst = t.mm(pscr, wchd[:, 0 * 384:0 * 384 + 128],
                                xckr[:, ci * LCH:(ci + 1) * LCH], ci == 0, ci == NCI - 1)
                    prjc.inc(inst)
                prjc.wait(s)
                s.act(tA[:, 0:LCH], pscr, AF.Sigmoid)
                cs_.inc_seq(s)
                cs_.wait(y)
                y.dma(tB[:, 0:LCH], g_dram[cs][:, sl], wdma)
                dwait(v)
                cs_.wait(v)
                v.tt(tA[:, 0:LCH], tA[:, 0:LCH], vcacc[:, cs * LCH:(cs + 1) * LCH], ALU.mult)
                v.tt(tA[:, 0:LCH], tA[:, 0:LCH], x1s[:, cs * LCH:(cs + 1) * LCH], ALU.add)
                v.tt(tA[:, 0:LCH], tA[:, 0:LCH], tB[:, 0:LCH], ALU.mult)
                v.tt(x[cs][:, sl], x[cs][:, sl], tA[:, 0:LCH], ALU.add)
                cv.inc_seq(v)
                env.barrier()
            for ci in range(NCI):
                o = ci * (LCH + 1)
                v.copy(xbbuf[:, o:o + 1], xbbuf[:, o + LCH:o + LCH + 1])
            env.barrier()

        if STOP <= 3:
            finish_early()
            return nc

        ln_stats(prod_x, tms(2), tms(3))
        P4B, Q4B = bc[0], bc[1]
        bounce1(tms(2), P4B, "p4")
        bounce1(tms(3), Q4B, "q4")

        def xn1_chunk(ci, dst):
            v.tt(dst, x[ci][:, 0:T], P4B[:, 0:T], ALU.mult)
            v.tt(dst, dst, Q4B[:, 0:T], ALU.subtract)

        y.dma(colsS[0:1, 48:50], sinv, wdma)
        dwait()
        env.barrier()
        bounce_scalar_to_col(colsS[0:1, 48:49], colsS[:, 4:5], "gsx")
        bounce_scalar_to_col(colsS[0:1, 49:50], colsS[:, 5:6], "gsd")

        def blk3(ap, nb):
            return ap.rearrange("p (b j) -> p b j", j=16)

        def sc8_of(src_ap, width, gs_col, sc8_out, tmp1, tmp2, rsc8_out):
            nb = width // 16
            v.red(sc8_out, blk3(src_ap, nb), ALU.max, absolute=True)
            v.ts(sc8_out, sc8_out, gs_col, None, ALU.mult)
            v.ts(sc8_out, sc8_out, 1.0 / 6.0, 448.0, ALU.mult, ALU.min)
            emit_e4m3(env, sc8_out, sc8_out, tmp1, tmp2)
            v.ts(rsc8_out, sc8_out, 1e-30, None, ALU.max)
            v.recip(rsc8_out, rsc8_out)

        def nvfp4_weight_pass(w_dram, q_dram, nchunks, width, gsw_col_idx):
            half = min(width, 2048)
            nhalf = width // half
            for ci in range(nchunks):
                for hh in range(nhalf):
                    wbv = A[4][:, 0:half]
                    env.barrier()
                    y.dma(wbv, w_dram[ci * 128:(ci + 1) * 128, hh * half:(hh + 1) * half], wdma)
                    dwait()
                    env.barrier()
                    v.red(colsS[:, 20:21], wbv, ALU.max, absolute=True)
                    if ci == 0 and hh == 0:
                        v.copy(colsS[:, 18:19], colsS[:, 20:21])
                    else:
                        v.tt(colsS[:, 18:19], colsS[:, 18:19], colsS[:, 20:21], ALU.max)
            env.barrier()
            inst = t.trans(pst2[0:1, 0:128], colsS[:, 18:19], identf[:])
            mmc.inc(inst)
            mmc.wait(v)
            v.red(colsS[0:1, 21:22], pst2[0:1, 0:128], ALU.max)
            v.recip(colsS[0:1, 21:22], colsS[0:1, 21:22])
            v.ts(colsS[0:1, 21:22], colsS[0:1, 21:22], 448.0 * 6.0, None, ALU.mult)
            env.barrier()
            bounce_scalar_to_col(colsS[0:1, 21:22], colsS[:, gsw_col_idx:gsw_col_idx + 1], f"gsw{gsw_col_idx}")
            gsw_col = colsS[:, gsw_col_idx:gsw_col_idx + 1]
            for ci in range(nchunks):
                for hh in range(nhalf):
                    wbv = A[4][:, 0:half]
                    env.barrier()
                    y.dma(wbv, w_dram[ci * 128:(ci + 1) * 128, hh * half:(hh + 1) * half], wdma)
                    dwait()
                    env.barrier()
                    nb = half // 16
                    sc8 = tmst[:, 0:nb]
                    rsc8 = A[5][:, 0:nb]
                    sc8_of(wbv, half, gsw_col, sc8, tA[:, 0:nb], tB[:, 0:nb], rsc8)
                    env.barrier()
                    emit_digitize_q_sc8(env, B[7][:, 0:half].rearrange("p (b j) -> p b j", j=16),
                                        blk3(wbv, nb),
                                        sc8.rearrange("p (b o) -> p b o", o=1).broadcast_to((128, nb, 16)),
                                        gsw_col,
                                        tA[:, 0:half].rearrange("p (b j) -> p b j", j=16),
                                        tB[:, 0:half].rearrange("p (b j) -> p b j", j=16),
                                        tC[:, 0:half].rearrange("p (b j) -> p b j", j=16),
                                        rsc8.rearrange("p (b o) -> p b o", o=1).broadcast_to((128, nb, 16)))
                    cv.inc_seq(v)
                    cv.wait(y)
                    y.dma(q_dram[ci * 128:(ci + 1) * 128, hh * half:(hh + 1) * half], B[7][:, 0:half], wdma)
                    dwait()
                    env.barrier()

        nvfp4_weight_pass(gw, gq_dram, NHF, C, 6)
        nvfp4_weight_pass(uw, uq_dram, NHF, C, 7)
        nvfp4_weight_pass(dw, dq_dram, NCI, HF, 8)

        for (src, gsc, dst) in ((6, 4, 9), (7, 4, 10), (8, 5, 11)):
            v.tt(colsS[:, dst:dst + 1], colsS[:, src:src + 1], colsS[:, gsc:gsc + 1], ALU.mult)
        v.recip(colsS[:, 9:10], colsS[:, 9:10])
        v.recip(colsS[:, 10:11], colsS[:, 10:11])
        v.recip(colsS[:, 11:12], colsS[:, 11:12])
        env.barrier()

        xqop = [B[ci // 4][:, (ci % 4) * 512:(ci % 4 + 1) * 512] for ci in range(NCI)]
        hqT = [B[2 + i // 4][:, (i % 4) * 512:(i % 4 + 1) * 512] for i in range(16)]
        trc = env.chan("trc")
        tre = env.chan("tre")
        for ts in range(NTS):
            sl = slice(ts * 512, (ts + 1) * 512)
            for tc in range(4):
                tcg = ts * 4 + tc
                for ci in range(NCI):
                    v.tt(tB[:, 0:128], x[ci][:, tcg * 128:(tcg + 1) * 128], P4B[:, tcg * 128:(tcg + 1) * 128], ALU.mult)
                    v.tt(tB[:, 0:128], tB[:, 0:128], Q4B[:, tcg * 128:(tcg + 1) * 128], ALU.subtract)
                    cv.inc_seq(v)
                    cv.wait(t)
                    slot = ci % 2
                    if ci >= 2:
                        tre.wait(t, tre.count - 1)
                    inst = t.trans(pbig[:, 3072 + slot * 512:3072 + slot * 512 + 128], tB[:, 0:128], identf[:])
                    trc.inc(inst)
                    trc.wait(v)
                    inst2 = v.copy(A[4][:, ci * 128:(ci + 1) * 128], pbig[:, 3072 + slot * 512:3072 + slot * 512 + 128])
                    tre.inc(inst2)
                env.barrier()
                nb = C // 16
                sc8 = tmst[:, 0:nb]
                rsc8 = A[5][:, 0:nb]
                sc8_of(A[4][:, 0:C], C, colsS[:, 4:5], sc8, tA[:, 0:nb], tB[:, 0:nb], rsc8)
                env.barrier()
                emit_digitize_q_sc8(env, B[6][:, 0:C].rearrange("p (b j) -> p b j", j=16),
                                    blk3(A[4][:, 0:C], nb),
                                    sc8.rearrange("p (b o) -> p b o", o=1).broadcast_to((128, nb, 16)),
                                    colsS[:, 4:5],
                                    tA[:, 0:C].rearrange("p (b j) -> p b j", j=16),
                                    tB[:, 0:C].rearrange("p (b j) -> p b j", j=16),
                                    tC[:, 0:C].rearrange("p (b j) -> p b j", j=16),
                                    rsc8.rearrange("p (b o) -> p b o", o=1).broadcast_to((128, nb, 16)))
                cv.inc_seq(v)
                cv.wait(y)
                for ci in range(NCI):
                    y.dma(xqop[ci][:, tc * 128:(tc + 1) * 128], B[6][:, ci * 128:(ci + 1) * 128],
                          wdma, transpose=True)
                dwait()
                env.barrier()
            for halfi in range(2):
                for osl in range(4):
                    os_ = halfi * 4 + osl
                    gq_ap = gq_dram[os_ * 512:(os_ + 1) * 512, :]
                    uq_ap = uq_dram[os_ * 512:(os_ + 1) * 512, :]
                    pg = [bank(i, 512) for i in range(4)]
                    pu = [bank(4 + i, 512) for i in range(4)]
                    for ci in range(NCI):
                        bi = ci % 2
                        prjc.wait(y)
                        y.dma(wq_small[bi][:, 0:512], gq_ap[:, ci * 128:(ci + 1) * 128], wdma, transpose=True)
                        y.dma(wq_small[bi][:, 512:1024], uq_ap[:, ci * 128:(ci + 1) * 128], wdma, transpose=True)
                        dwait(t)
                        for tc in range(4):
                            lhs = xqop[ci][:, tc * 128:(tc + 1) * 128]
                            t.mm(pg[tc], lhs, wq_small[bi][:, 0:512], ci == 0, ci == NCI - 1)
                            inst = t.mm(pu[tc], lhs, wq_small[bi][:, 512:1024], ci == 0, ci == NCI - 1)
                        prjc.inc(inst)
                    prjc.wait(s)
                    prjc.wait(v)
                    for tc in range(4):
                        s.act(tA[:, 0:512], pg[tc], AF.Silu, scale=colsS[:, 9:10])
                        v.ts(tB[:, 0:512], pu[tc], colsS[:, 10:11], None, ALU.mult)
                        cs_.inc_seq(s)
                        cs_.wait(v)
                        v.tt(tA[:, 0:512], tA[:, 0:512], tB[:, 0:512], ALU.mult)
                        nb = 512 // 16
                        sc8 = tmst[:, 0:nb]
                        rsc8 = A[5][:, 0:nb]
                        sc8_of(tA[:, 0:512], 512, colsS[:, 5:6], sc8, tB[:, 0:nb], tC[:, 0:nb], rsc8)
                        emit_digitize_q_sc8(env, B[6][:, 0:512].rearrange("p (b j) -> p b j", j=16),
                                            blk3(tA[:, 0:512], nb),
                                            sc8.rearrange("p (b o) -> p b o", o=1).broadcast_to((128, nb, 16)),
                                            colsS[:, 5:6],
                                            tB[:, 0:512].rearrange("p (b j) -> p b j", j=16),
                                            tC[:, 0:512].rearrange("p (b j) -> p b j", j=16),
                                            A[4][:, 0:512].rearrange("p (b j) -> p b j", j=16),
                                            rsc8.rearrange("p (b o) -> p b o", o=1).broadcast_to((128, nb, 16)))
                        cv.inc_seq(v)
                        cv.wait(y)
                        for j in range(4):
                            y.dma(hqT[osl * 4 + j][:, tc * 128:(tc + 1) * 128],
                                  B[6][:, j * 128:(j + 1) * 128], wdma, transpose=True)
                        dwait()
                        env.barrier()
                for cs in range(NCI):
                    pd = bank(cs % 2, 512)
                    dacc = A[6] if cs < 4 else A[7]
                    dseg = dacc[:, (cs % 4) * 512:(cs % 4 + 1) * 512]
                    vca.wait(t, max(0, vca.count - 1))
                    for jj in range(16):
                        hfs = halfi * 16 + jj
                        bi = jj % 2
                        wcv_pf.wait(y, max(0, wcv_pf.count - 1))
                        y.dma(wq_small[bi][:, 0:128],
                              dq_dram[cs * 128:(cs + 1) * 128, hfs * 128:(hfs + 1) * 128],
                              wdma, transpose=True)
                        dwait(t)
                        inst = t.mm(pd, wq_small[bi][:, 0:128], hqT[jj], jj == 0, jj == 15)
                        wcv_pf.inc(inst)
                    wcv_pf.wait(v)
                    if halfi == 0:
                        inst2 = v.copy(dseg, pd)
                    else:
                        inst2 = v.tt(dseg, dseg, pd, ALU.add)
                    vca.inc(inst2)
                env.barrier()
            for cs in range(NCI):
                dacc = A[6] if cs < 4 else A[7]
                dseg = dacc[:, (cs % 4) * 512:(cs % 4 + 1) * 512]
                v.ts(tA[:, 0:512], dseg, colsS[:, 11:12], None, ALU.mult)
                cv.inc_seq(v)
                cv.wait(s)
                s.act(tB[:, 0:512], tA[:, 0:512], AF.Silu)
                cs_.inc_seq(s)
                cs_.wait(y)
                y.dma(g1_dram[cs][:, sl], tB[:, 0:512], wdma)
                dwait()
                env.barrier()

        if STOP <= 5:
            finish_early()
            return nc

        COSB, SINB = bc[2], bc[3]
        rfc = colsS[:, 52:53]
        for blkk in range(4):
            y.dma(rfc[blkk * 32:(blkk + 1) * 32, 0:1], rotf.rearrange("o f -> (o f)")[:, None], wdma)
        dwait()
        env.barrier()
        TWO_PI = 2.0 * math.pi

        def range_reduce_sin(dst, ang_ap):
            v.ts(tC[:, 0:T], ang_ap, 1.0 / TWO_PI, None, ALU.mult)
            for qq in range(16):
                sq_ = slice(qq * 128, (qq + 1) * 128)
                v.copy(tI[:, 0:128], tC[:, sq_])
                v.copy(tC[:, sq_], tI[:, 0:128])
            v.ts(tC[:, 0:T], tC[:, 0:T], -TWO_PI, None, ALU.mult)
            v.tt(tC[:, 0:T], ang_ap, tC[:, 0:T], ALU.add)
            v.ts(tA[:, 0:T], tC[:, 0:T], math.pi, -TWO_PI, ALU.is_gt, ALU.mult)
            v.tt(tC[:, 0:T], tC[:, 0:T], tA[:, 0:T], ALU.add)
            v.ts(tA[:, 0:T], tC[:, 0:T], -math.pi, TWO_PI, ALU.is_lt, ALU.mult)
            v.tt(tC[:, 0:T], tC[:, 0:T], tA[:, 0:T], ALU.add)
            cv.inc_seq(v)
            cv.wait(s)
            s.act(dst, tC[:, 0:T], AF.Sin)
            cs_.inc_seq(s)
            cs_.wait(v)

        g.iota(tA[:, 0:T], [[1, T]], base=0, channel_multiplier=0)
        cg.inc_seq(g)
        cg.wait(v)
        dwait(v)
        v.ts(tB[:, 0:T], tA[:, 0:T], rfc, None, ALU.mult)
        range_reduce_sin(SINB[:, 0:T], tB[:, 0:T])
        v.ts(tB[:, 0:T], tB[:, 0:T], math.pi / 2.0, None, ALU.add)
        range_reduce_sin(COSB[:, 0:T], tB[:, 0:T])
        v.ts(SINB[0:32, 0:T], SINB[0:32, 0:T], -1.0, None, ALU.mult)
        v.ts(SINB[64:96, 0:T], SINB[64:96, 0:T], -1.0, None, ALU.mult)
        env.barrier()


        pbb = smallf[:, 0:1024]
        v.memset(onesr, 1.0)
        v.memset(onesr64, 1.0)
        env.barrier()

        NKT = T // 128
        ydbuf = A[1][:, 0:512]
        mqc = env.chan("mqc")
        eqc = env.chan("eqc")
        Qp, Kp, KKp, VKp = A[4], A[5], A[6], A[7]
        Qn = A[2]
        Vt = A[3]

        for hp in range(NPAIR):
            wofs = [hp * 128, C + hp * 128, 2 * C + hp * 128]
            kofs = [hp * 128, C + hp * 128]
            v.memset(Vt[:, 0:2080], 1.0)
            env.barrier()
            for tss in range(NTS):
                sl = slice(tss * 512, (tss + 1) * 512)
                pQ, pK, pKK, pVK = bank(0, 512), bank(1, 512), bank(2, 512), bank(3, 512)
                pV = [pbig[:, (4 + i) * 512:(4 + i) * 512 + 128] for i in range(4)]
                for ci in range(NCI):
                    bi = ci % 2
                    xn1_sl = tB[:, 0:512]
                    prjc.wait(v)
                    v.tt(xn1_sl, x[ci][:, sl], P4B[:, sl], ALU.mult)
                    v.tt(xn1_sl, xn1_sl, Q4B[:, sl], ALU.subtract)
                    cv.inc_seq(v)
                    prjc.wait(y)
                    y.dma(wchd[:, 0 * 384 + 0:0 * 384 + 128],
                          wqkv[ci * 128:(ci + 1) * 128, wofs[0]:wofs[0] + 128], wdma)
                    y.dma(wchd[:, 0 * 384 + 128:0 * 384 + 256],
                          wqkv[ci * 128:(ci + 1) * 128, wofs[1]:wofs[1] + 128], wdma)
                    y.dma(wchd[:, 0 * 384 + 256:0 * 384 + 384],
                          wkv2[ci * 128:(ci + 1) * 128, kofs[0]:kofs[0] + 128], wdma)
                    dwait(t)
                    cv.wait(t)
                    t.mm(pQ, wchd[:, 0 * 384 + 0:0 * 384 + 128], xn1_sl, ci == 0, ci == NCI - 1)
                    t.mm(pK, wchd[:, 0 * 384 + 128:0 * 384 + 256], xn1_sl, ci == 0, ci == NCI - 1)
                    inst = t.mm(pKK, wchd[:, 0 * 384 + 256:0 * 384 + 384], xn1_sl, ci == 0, ci == NCI - 1)
                    prjc.inc(inst)
                    prjc.wait(y)
                    y.dma(wchd[:, 0 * 384 + 0:0 * 384 + 128],
                          wkv2[ci * 128:(ci + 1) * 128, kofs[1]:kofs[1] + 128], wdma)
                    y.dma(wchd[:, 0 * 384 + 128:0 * 384 + 256],
                          wqkv[ci * 128:(ci + 1) * 128, wofs[2]:wofs[2] + 128], wdma)
                    dwait(t)
                    t.mm(pVK, wchd[:, 0 * 384 + 0:0 * 384 + 128], xn1_sl, ci == 0, ci == NCI - 1)
                    for tc in range(4):
                        inst = t.mm(pV[tc], xn1_sl[:, tc * 128:(tc + 1) * 128],
                                    wchd[:, 0 * 384 + 128:0 * 384 + 256], ci == 0, ci == NCI - 1)
                    prjc.inc(inst)
                prjc.wait(v)
                v.copy(Qp[:, sl], pQ)
                v.copy(Kp[:, sl], pK)
                v.copy(KKp[:, sl], pKK)
                v.copy(VKp[:, sl], pVK)
                for tc in range(4):
                    kt = tss * 4 + tc
                    v.copy(Vt[:, 0 * 1040 + kt * 65:0 * 1040 + kt * 65 + 64], pV[tc][:, 0:64])
                    v.copy(Vt[:, 1 * 1040 + kt * 65:1 * 1040 + kt * 65 + 64], pV[tc][:, 64:128])
                eqc.inc_seq(v)
                eqc.wait(t)
                env.barrier()
            for h in range(2):
                hb = h * 64
                g.memset(sct[hb:hb + 64, :], 0.0)
                g.drain()
                g.affine_select(sct[hb:hb + 64, :], sct[hb:hb + 64, :], ALU.not_equal, 8.0, 0, [[-1, 64]], 1)
                v.memset(colsS[0:1, 50 + h:51 + h], 4096.0)
            v.memset(tA[0:1, 768:1024], 0.0)
            env.barrier()
            for ch in range(NLQ):
                n0 = ch * LQ
                cs_l = slice(n0, n0 + LQ)
                for h in range(2):
                    hb = h * 64
                    rows = slice(hb, hb + 64)
                    Qh = Qp[rows, cs_l]
                    KKh = KKp[rows, cs_l]
                    VKh = VKp[rows, cs_l]
                    idh_h = identf[0:64, 0:64] if h == 0 else identf[64:128, 64:128]
                    pT0, pT1 = bank(0, 256), bank(1, 256)
                    t.mm(pT0, VKp[rows, n0:n0 + 128], Qh, True, True)
                    t.mm(pT1, VKp[rows, n0 + 128:n0 + 256], Qh, True, True)
                    t.trans(pbig[:, 4 * 512:4 * 512 + 64], KKp[rows, n0:n0 + 128], idh_h)
                    t.trans(pbig[:, 5 * 512:5 * 512 + 64], KKp[rows, n0 + 128:n0 + 256], idh_h)
                    t.trans(pbig[:, 6 * 512:6 * 512 + 64], VKp[rows, n0:n0 + 128], idh_h)
                    inst = t.trans(pbig[:, 7 * 512:7 * 512 + 64], VKp[rows, n0 + 128:n0 + 256], idh_h)
                    mqc.inc(inst)
                    mqc.wait(v)
                    v.copy(pbb[:, 0:256], pT0)
                    v.copy(pbb[:, 256:512], pT1)
                    v.copy(kktv[:, 0:64], pbig[:, 4 * 512:4 * 512 + 64])
                    v.copy(kktv[:, 64:128], pbig[:, 5 * 512:5 * 512 + 64])
                    v.copy(kktv[:, 128:192], pbig[:, 6 * 512:6 * 512 + 64])
                    v.copy(kktv[:, 192:256], pbig[:, 7 * 512:7 * 512 + 64])
                    cv.inc_seq(v)
                    cv.wait(g)
                    g.affine_select(pbb[:, 0:256], pbb[:, 0:256], ALU.is_ge, 0.0, 0, [[1, 256]], -1)
                    g.affine_select(pbb[:, 256:512], pbb[:, 256:512], ALU.is_ge, 0.0, -128, [[1, 256]], -1)
                    cg.inc_seq(g)
                    cg.wait(t)
                    eqc.inc_seq(v)
                    eqc.wait(t)
                    pQn = bank(2, 256)[0:64, :]
                    t.mm(pQn, sct[rows, :], Qh, True, False)
                    t.mm(pQn, kktv[:, 0:64], pbb[:, 0:256], False, False)
                    inst = t.mm(pQn, kktv[:, 64:128], pbb[:, 256:512], False, True)
                    mqc.inc(inst)
                    pG0, pG1 = bank(0, 256), bank(1, 256)
                    t.mm(pG0, KKp[rows, n0:n0 + 128], KKh, True, True)
                    inst = t.mm(pG1, KKp[rows, n0 + 128:n0 + 256], KKh, True, True)
                    mqc.inc(inst)
                    mqc.wait(v)
                    v.copy(pbb[:, 512:768], pG0)
                    v.copy(pbb[:, 768:1024], pG1)
                    eqc.inc_seq(v)
                    eqc.wait(t)
                    t.mm(pG0, VKp[rows, n0:n0 + 128], VKh, True, True)
                    inst = t.mm(pG1, VKp[rows, n0 + 128:n0 + 256], VKh, True, True)
                    mqc.inc(inst)
                    mqc.wait(v)
                    v.tt(pbb[:, 512:768], pbb[:, 512:768], pG0, ALU.mult)
                    v.tt(pbb[:, 768:1024], pbb[:, 768:1024], pG1, ALU.mult)
                    cv.inc_seq(v)
                    cv.wait(g)
                    g.affine_select(pbb[:, 512:768], pbb[:, 512:768], ALU.is_ge, 0.0, -1, [[1, 256]], -1)
                    g.affine_select(pbb[:, 768:1024], pbb[:, 768:1024], ALU.is_ge, 0.0, -129, [[1, 256]], -1)
                    cg.inc_seq(g)
                    cg.wait(t)
                    eqc.inc_seq(v)
                    eqc.wait(t)
                    pgr = bank(4, 256)
                    t.mm(pgr[0:1, :], ones_col[:], pbb[:, 512:768], True, False)
                    inst = t.mm(pgr[0:1, :], ones_col[:], pbb[:, 768:1024], False, True)
                    mqc.inc(inst)
                    cv.inc_seq(v)
                    cv.wait(s)
                    s.act(sqb[rows], KKh, AF.Square)
                    cs_.inc_seq(s)
                    cs_.wait(t)
                    pnk0 = bank(5, 256)
                    pnk1 = bank(6, 256)
                    inst = t.mm(pnk0[0:1, :], ones_col[rows, :], sqb[rows], True, True)
                    mqc.inc(inst)
                    mqc.wait(s)
                    s.act(sqb[rows], VKh, AF.Square)
                    cs_.inc_seq(s)
                    cs_.wait(t)
                    inst = t.mm(pnk1[0:1, :], ones_col[rows, :], sqb[rows], True, True)
                    mqc.inc(inst)
                    pSv0 = bank(7, 64)
                    pSv1 = bank(3, 64)
                    t.mm(pSv0, VKp[rows, n0:n0 + 128], sct[rows, :], True, True)
                    inst = t.mm(pSv1, VKp[rows, n0 + 128:n0 + 256], sct[rows, :], True, True)
                    mqc.inc(inst)
                    mqc.wait(v)
                    v.tt(sqb[:, 0:64], kktv[:, 0:64], pSv0, ALU.mult)
                    v.tt(sqb[:, 64:128], kktv[:, 64:128], pSv1, ALU.mult)
                    v.red(sqb[:, 128:129], sqb[:, 0:64], ALU.add)
                    v.red(sqb[:, 129:130], sqb[:, 64:128], ALU.add)
                    eqc.inc_seq(v)
                    eqc.wait(t)
                    pcr0 = bank(0, 128)
                    pcr1 = bank(1, 128)
                    t.trans(pcr0[0:1, :], sqb[:, 128:129], identf[:])
                    inst = t.trans(pcr1[0:1, :], sqb[:, 129:130], identf[:])
                    mqc.inc(inst)
                    mqc.wait(v)
                    v.copy(tA[0:1, 512:768], pnk0[0:1, :])
                    v.tt(tA[0:1, 512:768], tA[0:1, 512:768], pnk1[0:1, :], ALU.mult)
                    v.copy(tA[0:1, 0:128], pcr0[0:1, :])
                    v.copy(tA[0:1, 128:256], pcr1[0:1, :])
                    v.ts(tA[0:1, 0:256], tA[0:1, 0:256], 2.0, None, ALU.mult)
                    v.stt(tA[0:1, 256:512], pgr[0:1, :], 2.0, tA[0:1, 0:256], ALU.mult, ALU.add)
                    v.tt(tA[0:1, 256:512], tA[0:1, 256:512], tA[0:1, 512:768], ALU.add)
                    v.scan(tA[0:1, 0:256], tA[0:1, 256:512], tA[0:1, 768:1024], colsS[0:1, 50 + h:51 + h], ALU.add, ALU.add)
                    v.copy(colsS[0:1, 50 + h:51 + h], tA[0:1, 255:256])
                    cv.inc_seq(v)
                    cv.wait(s)
                    s.act(tA[0:1, 0:256], tA[0:1, 0:256], AF.Ln)
                    cs_.inc_seq(s)
                    cs_.wait(v)
                    v.ts(tA[0:1, 0:256], tA[0:1, 0:256], 0.5, None, ALU.mult)
                    cv.inc_seq(v)
                    cv.wait(s)
                    s.act(tA[0:1, 0:256], tA[0:1, 0:256], AF.Exp)
                    cs_.inc_seq(s)
                    cs_.wait(v)
                    v.ts(tA[0:1, 0:256], tA[0:1, 0:256], 1e-8, None, ALU.add)
                    v.recip(tA[0:1, 0:256], tA[0:1, 0:256])
                    eqc.inc_seq(v)
                    eqc.wait(t)
                    prn = bank(4, 256)[0:64, :]
                    inst = t.mm(prn, onesr, tA[0:1, 0:256], True, True)
                    mqc.inc(inst)
                    mqc.wait(v)
                    v.copy(sqb[0:64, 0:256], prn)
                    v.tt(ydbuf[0:64, 0:256], pQn, sqb[0:64, 0:256], ALU.mult)
                    if h == 0:
                        v.copy(Qn[0:64, cs_l], ydbuf[0:64, 0:256])
                        cv.inc_seq(v)
                    else:
                        cv.inc_seq(v)
                        cv.wait(y)
                        y.dma(Qn[64:128, cs_l], ydbuf[0:64, 0:256], wdma)
                        dwait(v)
                    pSc = bank(5, 64)[0:64, :]
                    t.mm(pSc, kktv[:, 128:192], kktv[:, 0:64], True, False)
                    inst = t.mm(pSc, kktv[:, 192:256], kktv[:, 64:128], False, True)
                    mqc.inc(inst)
                    if h == 0:
                        mqc.wait(v)
                        v.tt(sct[0:64, :], sct[0:64, :], pSc[0:64, :], ALU.add)
                        eqc.inc_seq(v)
                        eqc.wait(t)
                    else:
                        mqc.wait(v)
                        v.copy(ydbuf[0:64, 0:64], pSc[0:64, :])
                        cv.inc_seq(v)
                        cv.wait(y)
                        y.dma(ydbuf[64:128, 0:64], ydbuf[0:64, 0:64], wdma)
                        dwait(v)
                        v.tt(sct[64:128, :], sct[64:128, :], ydbuf[64:128, 0:64], ALU.add)
                        eqc.inc_seq(v)
                        eqc.wait(t)
                    env.barrier()

            for src_t in (Qn, Kp):
                cv.inc_seq(v)
                cv.wait(y)
                for h in range(2):
                    e0 = h * 64
                    y.dma(tA[e0:e0 + 32, 0:T], src_t[e0 + 32:e0 + 64, 0:T], wdma)
                    y.dma(tA[e0 + 32:e0 + 64, 0:T], src_t[e0:e0 + 32, 0:T], wdma)
                dwait(v)
                v.tt(tB[:, 0:T], src_t[:, 0:T], COSB[:, 0:T], ALU.mult)
                v.tt(tA[:, 0:T], tA[:, 0:T], SINB[:, 0:T], ALU.mult)
                v.tt(src_t[:, 0:T], tB[:, 0:T], tA[:, 0:T], ALU.add)
            env.barrier()
            for h in range(2):
                rows = slice(h * 64, h * 64 + 64)
                cv.inc_seq(v)
                cv.wait(s)
                s.act(tA[rows, 0:T], Qn[rows, 0:T], AF.Square)
                s.act(tB[rows, 0:T], Kp[rows, 0:T], AF.Square)
                cs_.inc_seq(s)
                cs_.wait(t)
                for which, srcsq in ((0, tA), (1, tB)):
                    for qq in range(4):
                        pq2 = bank(qq % 2, 512)
                        eqc.wait(t)
                        inst = t.mm(pq2[0:1, :], ones_col[rows, :],
                                    srcsq[rows, qq * 512:(qq + 1) * 512], True, True)
                        mqc.inc(inst)
                        mqc.wait(v)
                        v.red(colsS[0:1, 56:57], pq2[0:1, :], ALU.max)
                        if qq == 0:
                            v.copy(colsS[0:1, 57 + which:58 + which], colsS[0:1, 56:57])
                        else:
                            v.tt(colsS[0:1, 57 + which:58 + which],
                                 colsS[0:1, 57 + which:58 + which], colsS[0:1, 56:57], ALU.max)
                        eqc.inc_seq(v)
                v.tt(colsS[0:1, 56:57], colsS[0:1, 57:58], colsS[0:1, 58:59], ALU.mult)
                v.ts(colsS[0:1, 56:57], colsS[0:1, 56:57], 1e-30, None, ALU.max)
                cv.inc_seq(v)
                cv.wait(s)
                s.act(colsS[0:1, 56:57], colsS[0:1, 56:57], AF.Ln)
                cs_.inc_seq(s)
                cs_.wait(v)
                v.ts(colsS[0:1, 56:57], colsS[0:1, 56:57], 0.5, None, ALU.mult)
                cv.inc_seq(v)
                cv.wait(s)
                s.act(colsS[0:1, 56:57], colsS[0:1, 56:57], AF.Exp)
                cs_.inc_seq(s)
                cs_.wait(v)
                v.ts(colsS[0:1, 56:57], colsS[0:1, 56:57], -0.125, None, ALU.mult)
                env.barrier()
                bounce_scalar_to_col(colsS[0:1, 56:57], colsS[:, 60 + h:61 + h], f"nb{hp}_{h}")

            for h in range(2):
                rows = slice(h * 64, h * 64 + 64)
                negb = colsS[:, 60 + h:61 + h]
                for qs in range(NTS):
                    qsl = slice(qs * 512, (qs + 1) * 512)
                    po = bank(2, 512)[0:65, :]
                    nkt = (qs + 1) * 4
                    for kt in range(nkt):
                        psc = bank(kt % 2, 512)
                        if kt >= 2:
                            eqc.wait(t, eqc.count)
                        inst = t.mm(psc, Kp[rows, kt * 128:(kt + 1) * 128], Qn[rows, qsl], True, True)
                        mqc.inc(inst)
                        mqc.wait(s)
                        pslot = pbb[:, (kt % 2) * 512:(kt % 2) * 512 + 512]
                        s.act(pslot, psc, AF.Exp, bias=negb, scale=0.125)
                        cs_.inc_seq(s)
                        if kt * 128 + 128 > qs * 512:
                            cs_.wait(g)
                            g.affine_select(pslot, pslot, ALU.is_ge, 0.0,
                                            qs * 512 - kt * 128, [[1, 512]], -1)
                            cg.inc_seq(g)
                            cg.wait(t)
                        else:
                            cs_.wait(t)
                        inst = t.mm(po, Vt[:, h * 1040 + kt * 65:h * 1040 + kt * 65 + 65],
                                    pslot, kt == 0, kt == nkt - 1)
                        mqc.inc(inst)
                        eqc.inc_seq(s)
                    mqc.wait(v)
                    v.recip(ydbuf[64:65, 0:512], po[64:65, :])
                    cv.inc_seq(v)
                    cv.wait(t)
                    prn = bank(4, 512)[0:64, :]
                    inst = t.mm(prn, onesr64, ydbuf[64:65, 0:512], True, True)
                    mqc.inc(inst)
                    mqc.wait(v)
                    v.copy(ydbuf[0:64, 0:512], prn)
                    v.tt(ydbuf[0:64, 0:512], po[0:64, :], ydbuf[0:64, 0:512], ALU.mult)
                    cv.inc_seq(v)
                    cv.wait(y)
                    y.dma(y_dram[hp * 128 + h * 64:hp * 128 + h * 64 + 64, qsl], ydbuf[0:64, 0:512], wdma)
                    dwait(v)
                    eqc.inc_seq(v)
                    eqc.wait(t)
                    env.barrier()
            env.barrier()

        for tss in range(NTS):
            sl = slice(tss * 512, (tss + 1) * 512)
            for hpj in range(NPAIR):
                dst = A[4] if hpj < 4 else A[5]
                y.dma(dst[:, (hpj % 4) * 512:(hpj % 4 + 1) * 512],
                      y_dram[hpj * 128:(hpj + 1) * 128, sl], wdma)
            dwait(t)
            env.barrier()
            for cs_i in range(NCI):
                ps = pmm.acquire(t)
                for hpj in range(NPAIR):
                    bi = hpj % 2
                    prjc.wait(y)
                    y.dma(wchd[:, 0 * 384:0 * 384 + 128],
                          wout[hpj * 128:(hpj + 1) * 128, cs_i * 128:(cs_i + 1) * 128], wdma)
                    dwait(t)
                    src_y = A[4] if hpj < 4 else A[5]
                    inst = t.mm(ps, wchd[:, 0 * 384:0 * 384 + 128],
                                src_y[:, (hpj % 4) * 512:(hpj % 4 + 1) * 512],
                                hpj == 0, hpj == NPAIR - 1)
                    prjc.inc(inst)
                prjc.wait(y)
                y.dma(tB[:, 0:512], g1_dram[cs_i][:, sl], wdma)
                dwait(v)
                prjc.wait(v)
                v.ts(tA[:, 0:512], ps, boutc[:, cs_i:cs_i + 1], None, ALU.add)
                v.tt(tA[:, 0:512], tA[:, 0:512], tB[:, 0:512], ALU.mult)
                v.tt(x[cs_i][:, sl], x[cs_i][:, sl], tA[:, 0:512], ALU.add)
                evc.inc_seq(v)
                pmm.release(evc)
                cv.inc_seq(v)
                cv.wait(y)
                y.dma(outT[cs_i * 128:(cs_i + 1) * 128, sl], x[cs_i][:, sl], wdma)
            dwait()
            env.barrier()
        dwait()
        env.barrier()
    return nc



_BUILD_CACHE = {}


def _get_nc():
    if "nc" not in _BUILD_CACHE:
        _BUILD_CACHE["nc"] = build_program()
    return _BUILD_CACHE["nc"]


def _prep_inputs(inp):
    f32 = np.float32
    T, C, HF, H, HD = 2048, 1024, 4096, 16, 64
    c = lambda a: np.ascontiguousarray(a, dtype=f32)
    perm64 = np.concatenate([np.arange(0, HD, 2), np.arange(1, HD, 2)])
    permQ = np.concatenate([h * HD + perm64 for h in range(H)])
    wqkv = np.asarray(inp["L1_attn_w_qkv"], f32)
    wkvkv = np.asarray(inp["L1_attn_w_kvkv"], f32)
    base = {
        "wkT": c(np.asarray(inp["L0_attn_Wk"], f32).T),
        "wvT": c(np.asarray(inp["L0_attn_Wv"], f32).T),
        "wrT": c(np.asarray(inp["L0_attn_Wr"], f32).T),
        "wckT": c(np.asarray(inp["L0_attn_Wck"], f32).T),
        "wcvT": c(np.asarray(inp["L0_attn_Wcv"], f32).T),
        "wcrT": c(np.asarray(inp["L0_attn_Wcr"], f32).T),
        "vecs": c(np.stack([np.asarray(inp[k], f32) for k in (
            "L0_attn_w", "L0_attn_u", "L0_attn_rk", "L0_attn_rv", "L0_attn_rr",
            "L0_attn_crk", "L0_attn_crr", "L0_attn_ln1_g", "L0_attn_ln1_b",
            "L0_attn_ln2_g", "L0_attn_ln2_b")] + [np.zeros(C, f32)], 0)),
        "w1T": c(np.asarray(inp["L0_ffn_w1"], f32).T),
        "w2T": c(np.asarray(inp["L0_ffn_w2"], f32).T),
        "convw": c(np.asarray(inp["L0_ffn_conv_w"], f32).reshape(HF, 3)),
        "convb": c(np.asarray(inp["L0_ffn_conv_b"], f32).reshape(HF, 1)),
        "wqkv": c(np.concatenate([wqkv[:, permQ], wqkv[:, C + permQ], wqkv[:, 2 * C:3 * C]], 1)),
        "wkv2": c(np.concatenate([wkvkv[:, permQ], wkvkv[:, C + permQ]], 1)),
        "wout": c(np.asarray(inp["L1_attn_w_out"], f32)),
        "boutv": c(np.asarray(inp["L1_attn_b_out"], f32).reshape(1, C)),
        "gw": c(np.asarray(inp["L1_ffn_gate_w"], f32)),
        "uw": c(np.asarray(inp["L1_ffn_up_w"], f32)),
        "dw": c(np.asarray(inp["L1_ffn_down_w"], f32)),
        "sinv": c(np.array([[float(np.asarray(inp["L1_ffn_s_in"]).reshape(-1)[0]),
                             float(np.asarray(inp["L1_ffn_s_in_down"]).reshape(-1)[0])]], f32)),
        "rotf": c((1.0 / 10000.0 ** (np.arange(0, HD, 2, dtype=np.float64) / HD)).reshape(1, HD // 2)),
    }
    xfull = np.asarray(inp["L0_attn_x"], f32)
    maps = []
    for b in range(8):
        m = dict(base)
        m["xT"] = c(xfull[b].T)
        maps.append(m)
    return maps


def kernel(**inputs):
    nc = _get_nc()
    maps = _prep_inputs(inputs)
    res = run_bass_kernel_spmd(nc, maps, core_ids=list(range(8)))
    out = np.stack([np.asarray(res.results[b]["outT"]).T for b in range(8)], 0)
    return np.ascontiguousarray(out, dtype=np.float32)
